# version 01
# baseline (speedup 1.0000x reference)
import math
from contextlib import ExitStack

import numpy as np
import ml_dtypes

import concourse.bass as bass
import concourse.mybir as mybir
from concourse.bass_utils import run_bass_kernel_spmd

F32 = mybir.dt.float32
BF16 = mybir.dt.bfloat16
AF = mybir.ActivationFunctionType
ALU = mybir.AluOpType
AX = mybir.AxisListType

BF = ml_dtypes.bfloat16

B, T, D = 2, 8192, 2048
NH, DH = 16, 128
NT = T // 128
TPC = 16
NCORES = 8
NEG = -1e30
EPS_LN = 1e-5
EPS_RMS = 1e-6
SCALE_ATTN = DH ** -0.5
CAP = 50.0

ENGINES = {"sp": "sync", "pe": "tensor", "act": "scalar", "dve": "vector", "pool": "gpsimd"}



class Sched:

    def __init__(self):
        self.items = []
        self.cnt = {e: 0 for e in ENGINES}
        self.waited = {}

    SERIALIZE = (("pe", "dve"), ("pe", "act"))

    def op(self, eng, fn, deps=(), inc=1):
        for (ce, pe_) in Sched.SERIALIZE:
            if ce == eng:
                deps = list(deps) + [(pe_, self.cnt[pe_])]
        waits = []
        for tok in deps:
            if tok is None:
                continue
            seng, sval = tok
            if sval <= 0:
                continue
            k = (eng, seng)
            if self.waited.get(k, -1) < sval:
                waits.append((seng, sval))
                self.waited[k] = sval
        self.items.append((eng, fn, waits, inc))
        self.cnt[eng] += inc
        return (eng, self.cnt[eng])

    def dma(self, eng, out, in_, deps=()):
        return self.op(eng, lambda e, o=out, i=in_: e.dma_start(out=o, in_=i), deps, inc=16)

    def tok(self, eng):
        return (eng, self.cnt[eng])


def replay(nc, sched, sems, block):
    def mk(eng_key, engobj):
        for (eng, fn, waits, inc) in sched.items:
            if eng != eng_key:
                continue
            for (seng, sval) in waits:
                engobj.wait_ge(sems[seng], sval)
            if fn is None:
                continue
            inst = fn(engobj)
            if inc and inst is not None:
                inst.then_inc(sems[eng], inc)

    @block.sync
    def _(e):
        mk("sp", e)

    @block.tensor
    def _(e):
        mk("pe", e)

    @block.scalar
    def _(e):
        mk("act", e)

    @block.vector
    def _(e):
        mk("dve", e)

    @block.gpsimd
    def _(e):
        mk("pool", e)


def finish(nc, s, sems_stack):
    s.op("pool", None, deps=[s.tok("pool"), s.tok("sp")], inc=0)
    with nc.Block() as block:
        replay(nc, s, sems_stack, block)



def ln_stats(s, xs, m, var, sq, ncols, eps, deps):
    t0 = s.op("dve", lambda e: e.reduce_sum(m[:, 0:1], xs[:, :ncols], axis=AX.X), deps=deps)
    t1 = s.op("act", lambda e: e.mul(m[:, 0:1], m[:, 0:1], 1.0 / ncols), deps=[t0])
    t2 = s.op("dve", lambda e: e.tensor_scalar(sq[:, :ncols], xs[:, :ncols], m[:, 0:1], None,
                                               ALU.subtract), deps=[t1] + list(deps))
    t3 = s.op("dve", lambda e: e.tensor_mul(sq[:, :ncols], sq[:, :ncols], sq[:, :ncols]), deps=[t2])
    t4 = s.op("dve", lambda e: e.reduce_sum(var[:, 0:1], sq[:, :ncols], axis=AX.X), deps=[t3])
    t5 = s.op("dve", lambda e: e.tensor_scalar(var[:, 0:1], var[:, 0:1], 1.0 / ncols, eps,
                                               ALU.mult, ALU.add), deps=[t4])
    t6 = s.op("act", lambda e: e.activation(var[:, 0:1], var[:, 0:1], AF.Sqrt), deps=[t5])
    t7 = s.op("dve", lambda e: e.reciprocal(var[:, 1:2], var[:, 0:1]), deps=[t6])
    return t2, t7


def emit_ln(s, xs, out, m, var, sq, ncols, eps, deps, out_bf=False):
    _, t7 = ln_stats(s, xs, m, var, sq, ncols, eps, deps)
    tn = s.op("dve", lambda e: e.tensor_scalar(out[:, :ncols], xs[:, :ncols], m[:, 0:1], var[:, 1:2],
                                               ALU.subtract, ALU.mult), deps=[t7] + list(deps))
    return tn



def _tile_lhsT(w):
    return np.ascontiguousarray(
        w.reshape(16, 128, 64, 128).transpose(2, 1, 0, 3).reshape(64, 128, 2048))


def host_prep(inp, seed_perm=None):
    h = {}
    perm = (np.arange(8192).reshape(2048, 4).T).reshape(-1)
    h["perm"] = perm

    f32 = np.float32
    w = {}
    w["l0_wqT"] = np.ascontiguousarray(inp["L0_attn_wq"].T)
    w["l0_wkT"] = np.ascontiguousarray(inp["L0_attn_wk"].T)
    w["l0_wvT"] = np.ascontiguousarray(inp["L0_attn_wv"].T)
    w["l0_woT"] = np.ascontiguousarray(inp["L0_attn_wo"].T)
    w["l0_wg"] = _tile_lhsT(inp["L0_ffn_w_gate"][:, perm])
    w["l0_wu"] = _tile_lhsT(inp["L0_ffn_w_up"][:, perm])
    w["l0_wd"] = np.ascontiguousarray(inp["L0_ffn_w_down"][perm, :])
    w["l1_wqT"] = np.ascontiguousarray(inp["L1_attn_w_q"].T)
    w["l1_wkT"] = np.ascontiguousarray(inp["L1_attn_w_k"].T)
    w["l1_wvT"] = np.ascontiguousarray(inp["L1_attn_w_v"].T)
    w["l1_woT"] = np.ascontiguousarray(inp["L1_attn_w_o"].T)
    w["l1_wg"] = _tile_lhsT(np.ascontiguousarray(inp["L1_ffn_w_gate"].T))
    w["l1_wu"] = _tile_lhsT(np.ascontiguousarray(inp["L1_ffn_w_up"].T))
    w["l1_wd"] = np.ascontiguousarray(inp["L1_ffn_w_down"].T)
    h["w_f32"] = w

    c = inp["L0_attn_rope_cos"][:NH]
    sn = inp["L0_attn_rope_sin"][:NH]
    C0 = np.repeat(c, 2, axis=1).reshape(-1)[None, :].repeat(128, 0)
    S0 = np.repeat(sn, 2, axis=1).reshape(-1)[None, :].repeat(128, 0)
    h["C0"] = np.ascontiguousarray(C0, dtype=f32)
    h["S0"] = np.ascontiguousarray(S0, dtype=f32)
    h["qnw"] = np.ascontiguousarray(np.tile(inp["L0_attn_q_norm_w"], (128, NH)), dtype=f32)
    h["knw"] = np.ascontiguousarray(np.tile(inp["L0_attn_k_norm_w"], (128, NH)), dtype=f32)

    t = np.arange(T, dtype=np.float64)
    inv = 1.0 / (10000.0 ** (np.arange(0, DH, 2, dtype=np.float64) / DH))
    fr = np.outer(t, inv)
    emb = np.concatenate([fr, fr], axis=1)
    h["cos1"] = np.cos(emb).astype(f32)
    h["sin1"] = np.sin(emb).astype(f32)

    h["ident_f"] = np.eye(128, dtype=f32)
    h["ident_b"] = np.eye(128, dtype=BF)
    return h


def shard_rows(arr_bt, qq):
    tiles = arr_bt.reshape(NT, 128, *arr_bt.shape[1:])
    return np.ascontiguousarray(tiles[[4 * i + qq for i in range(TPC)]].reshape(TPC * 128, *arr_bt.shape[1:]))


def unshard_rows(parts):
    out = np.empty((NT, 128) + parts[0].shape[1:], dtype=parts[0].dtype)
    for qq in range(4):
        tiles = parts[qq].reshape(TPC, 128, *parts[qq].shape[1:])
        for i in range(TPC):
            out[4 * i + qq] = tiles[i]
    return out.reshape(T, *parts[0].shape[1:])



def build_castw(nc):
    wf = nc.dram_tensor("wf", [16, 128, 8192], F32, kind="ExternalInput").ap()
    wb = nc.dram_tensor("wb", [16, 128, 8192], BF16, kind="ExternalOutput").ap()
    s = Sched()
    with ExitStack() as st:
        sems = {e: st.enter_context(nc.semaphore(f"tk_{e}")) for e in ENGINES}
        stg = [st.enter_context(nc.sbuf_tensor(f"stg{i}", [128, 8192], F32)) for i in range(2)]
        ob = [st.enter_context(nc.sbuf_tensor(f"ob{i}", [128, 8192], BF16)) for i in range(2)]
        last = [None, None]
        for i in range(16):
            b = i % 2
            ld = s.dma("sp", stg[b][:, :], wf[i, :, :], deps=[last[b]])
            cp = s.op("dve", lambda e, b=b: e.tensor_copy(ob[b][:, :], stg[b][:, :]), deps=[ld])
            last[b] = s.dma("pool", wb[i, :, :], ob[b][:, :], deps=[cp])
        finish(nc, s, sems)
    return nc


def run_castw(h):
    mats = ["l0_wqT", "l0_wkT", "l0_wvT", "l0_woT", "l0_wg", "l0_wu", "l0_wd",
            "l1_wqT", "l1_wkT", "l1_wvT", "l1_woT", "l1_wg", "l1_wu", "l1_wd"]
    flat = np.concatenate([h["w_f32"][m].reshape(-1) for m in mats])
    assert flat.size == 1 << 27
    per = flat.reshape(NCORES, 16, 128, 8192)
    nc = bass.Bass("TRN2", target_bir_lowering=False)
    build_castw(nc)
    res = run_bass_kernel_spmd(nc, [{"wf": np.ascontiguousarray(per[c])} for c in range(NCORES)],
                               core_ids=list(range(NCORES)))
    flatb = np.concatenate([r["wb"].reshape(-1) for r in res.results])
    out = {}
    off = 0
    for m in mats:
        a = h["w_f32"][m]
        out[m] = flatb[off:off + a.size].reshape(a.shape)
        off += a.size
    return out


KERNEL_N_LAUNCH = 0



def build_qkv(nc, layer):
    x = nc.dram_tensor("x", [2048, 2048], F32, kind="ExternalInput").ap()
    wqT = nc.dram_tensor("wqT", [2048, 2048], BF16, kind="ExternalInput").ap()
    wkT = nc.dram_tensor("wkT", [2048, 2048], BF16, kind="ExternalInput").ap()
    wvT = nc.dram_tensor("wvT", [2048, 2048], BF16, kind="ExternalInput").ap()
    identf = nc.dram_tensor("identf", [128, 128], F32, kind="ExternalInput").ap()
    if layer == 0:
        C0 = nc.dram_tensor("C0", [128, 2048], F32, kind="ExternalInput").ap()
        S0 = nc.dram_tensor("S0", [128, 2048], F32, kind="ExternalInput").ap()
        qnw = nc.dram_tensor("qnw", [128, 2048], F32, kind="ExternalInput").ap()
        knw = nc.dram_tensor("knw", [128, 2048], F32, kind="ExternalInput").ap()
    else:
        cosw = nc.dram_tensor("cosw", [2048, 2048], F32, kind="ExternalInput").ap()
        sinw = nc.dram_tensor("sinw", [2048, 2048], F32, kind="ExternalInput").ap()
    qt = nc.dram_tensor("qt", [NH, TPC, 128, 128], BF16, kind="ExternalOutput").ap()
    kt = nc.dram_tensor("kt", [NH, TPC, 128, 128], BF16, kind="ExternalOutput").ap()
    vt = nc.dram_tensor("vt", [TPC, NH, 128, 128], BF16, kind="ExternalOutput").ap()

    s = Sched()
    with ExitStack() as st:
        sems = {e: st.enter_context(nc.semaphore(f"tk_{e}")) for e in ENGINES}
        sb = lambda name, shp, dt=F32: st.enter_context(nc.sbuf_tensor(name, shp, dt))
        xs = [sb(f"xs{i}", [128, 2048]) for i in range(2)]
        xn = sb("xn", [128, 2048])
        sq = sb("sqs", [128, 2048])
        mm = sb("mm", [128, 2])
        vv = sb("vvs", [128, 2])
        cc = sb("cc", [128, 2])
        idf = sb("idf", [128, 128])
        xnT = sb("xnT", [128, TPC * 2048], BF16)
        wsb = sb("wsb", [128, 16 * 2048], BF16)
        qsb = sb("qsb", [128, 2048])
        qn2 = xn
        rq = sb("rq", [128, 2048])
        vb = sb("vb", [128, 2048], BF16)
        rms16 = sb("rms16", [128, 32])
        tb = [sb(f"tb{i}", [128, 128], BF16) for i in range(4)]
        if layer == 0:
            c0s = sb("c0s", [128, 2048])
            s0s = sb("s0s", [128, 2048])
            nws = sb("nws", [128, 2048])
        else:
            coss = sb("coss", [128, 2048])
            sins = sb("sins", [128, 2048])
        pp = [st.enter_context(nc.psum_tensor(f"pp{i}", [128, 512], F32)) for i in range(2)]
        trp = [st.enter_context(nc.psum_tensor(f"trp{i}", [128, 512], F32)) for i in range(2)]

        cst = s.op("dve", lambda e: e.memset(cc[:, 0:1], EPS_RMS))
        ldi = s.dma("sp", idf[:, :], identf[:, :])

        pe_prev = None
        tr_evac = [None, None]
        ld_prev = [None, None]
        for tt in range(TPC):
            b = tt % 2
            ldx = s.dma("sp", xs[b][:, :], x[tt * 128:(tt + 1) * 128, :], deps=[ld_prev[b]])
            tln = emit_ln(s, xs[b], xn, mm, vv, sq, 2048, EPS_LN, [ldx, pe_prev])
            ld_prev[b] = tln
            for k in range(16):
                bk = k % 2
                tp = s.op("pe", lambda e, k=k, bk=bk: e.transpose(
                    trp[bk][:, 0:128], xn[:, k * 128:(k + 1) * 128], idf[:, :]),
                    deps=[tln, ldi, tr_evac[bk]])
                ev = s.op("dve", lambda e, tt=tt, k=k, bk=bk: e.tensor_copy(
                    xnT[:, tt * 2048 + k * 128: tt * 2048 + (k + 1) * 128], trp[bk][:, 0:128]),
                    deps=[tp])
                tr_evac[bk] = ev
                if k == 15:
                    pe_prev = tp

        allx = s.tok("dve")

        def hv(buf, lo):
            v = buf[:, :].rearrange("p (h d) -> p h d", d=128)
            return v[:, :, 0:64] if lo else v[:, :, 64:128]

        def proj_phase(wT, post):
            ldw = [s.dma("sp", wsb[:, k * 2048:(k + 1) * 2048], wT[k * 128:(k + 1) * 128, :],
                         deps=[s.tok("pe")] if k == 0 else []) for k in range(16)]
            bank_last = [None, None]
            for tt in range(TPC):
                evs = []
                for n in range(4):
                    bk = n % 2
                    tokmm = None
                    for k in range(16):
                        tokmm = s.op("pe", lambda e, tt=tt, n=n, k=k, bk=bk: e.matmul(
                            pp[bk][:, :],
                            xnT[:, tt * 2048 + k * 128: tt * 2048 + (k + 1) * 128],
                            wsb[:, k * 2048 + n * 512: k * 2048 + n * 512 + 512],
                            start=(k == 0), stop=(k == 15)),
                            deps=[allx, ldw[15], bank_last[bk]] if k == 0 else [],
                            inc=(1 if k == 15 else 0))
                    ev = s.op("dve", lambda e, n=n, bk=bk: e.tensor_copy(
                        qsb[:, n * 512:(n + 1) * 512], pp[bk][:, :]), deps=[tokmm])
                    bank_last[bk] = ev
                    evs.append(ev)
                post(tt, evs)

        st_prev = [None]
        rope_deps = [[]]

        def qk_post(tt, evs, out_dram):
            deps = list(evs)
            if layer == 0:
                t1 = s.op("dve", lambda e: e.tensor_mul(sq[:, :], qsb[:, :], qsb[:, :]), deps=deps)
                t2 = s.op("dve", lambda e: e.reduce_sum(
                    rms16[:, 0:16], sq[:, :].rearrange("p (h d) -> p h d", d=128), axis=AX.X), deps=[t1])
                t3 = s.op("act", lambda e: e.activation(rms16[:, 16:32], rms16[:, 0:16], AF.Sqrt,
                                                        bias=cc[:, 0:1], scale=1.0 / 128), deps=[t2, cst])
                t4 = s.op("dve", lambda e: e.reciprocal(rms16[:, 0:16], rms16[:, 16:32]), deps=[t3])
                hd = []
                for h in range(NH):
                    hd.append(s.op("dve", lambda e, h=h: e.tensor_scalar_mul(
                        qn2[:, h * 128:(h + 1) * 128], qsb[:, h * 128:(h + 1) * 128], rms16[:, h:h + 1]),
                        deps=[t4]))
                t5 = s.op("dve", lambda e: e.tensor_mul(qn2[:, :], qn2[:, :], nws[:, :]),
                          deps=hd + rope_deps[0])
                r1 = s.op("dve", lambda e: e.tensor_mul(rq[:, :], qn2[:, :], c0s[:, :]), deps=[t5])
                r2 = s.op("dve", lambda e: e.tensor_mul(sq[:, 0:2048:2], qn2[:, 1:2048:2], s0s[:, 0:2048:2]),
                          deps=[t5])
                r3 = s.op("dve", lambda e: e.tensor_sub(rq[:, 0:2048:2], rq[:, 0:2048:2], sq[:, 0:2048:2]),
                          deps=[r1, r2])
                r4 = s.op("dve", lambda e: e.tensor_mul(sq[:, 1:2048:2], qn2[:, 0:2048:2], s0s[:, 1:2048:2]),
                          deps=[t5])
                rtok = s.op("dve", lambda e: e.tensor_add(rq[:, 1:2048:2], rq[:, 1:2048:2], sq[:, 1:2048:2]),
                            deps=[r3, r4])
            else:
                ldc = s.dma("sp", coss[:, :], cosw[tt * 128:(tt + 1) * 128, :], deps=deps)
                lds = s.dma("sp", sins[:, :], sinw[tt * 128:(tt + 1) * 128, :], deps=[])
                r1 = s.op("dve", lambda e: e.tensor_mul(rq[:, :], qsb[:, :], coss[:, :]),
                          deps=deps + [ldc])
                r2 = s.op("dve", lambda e: e.tensor_mul(hv(sq, True), hv(qsb, False), hv(sins, True)),
                          deps=deps + [lds])
                r3 = s.op("dve", lambda e: e.tensor_sub(hv(rq, True), hv(rq, True), hv(sq, True)),
                          deps=[r1, r2])
                r4 = s.op("dve", lambda e: e.tensor_mul(hv(sq, False), hv(qsb, True), hv(sins, False)),
                          deps=[r2])
                rtok = s.op("dve", lambda e: e.tensor_add(hv(rq, False), hv(rq, False), hv(sq, False)),
                            deps=[r3, r4])
            tr_ev = [None, None]
            sts = []
            for k in range(16):
                bk = k % 2
                tp = s.op("pe", lambda e, k=k, bk=bk: e.transpose(
                    trp[bk][:, 0:128], rq[:, k * 128:(k + 1) * 128], idf[:, :]),
                    deps=[rtok, tr_ev[bk]])
                ev = s.op("dve", lambda e, k=k, bk=bk: e.tensor_copy(tb[k % 4][:, :], trp[bk][:, 0:128]),
                          deps=[tp] + ([sts[k - 4]] if k >= 4 else [st_prev[0]]))
                tr_ev[bk] = ev
                sts.append(s.dma("pool", out_dram[k, tt, :, :], tb[k % 4][:, :], deps=[ev]))
            st_prev[0] = sts[-1]

        def v_post(tt, evs):
            cv = s.op("dve", lambda e: e.tensor_copy(vb[:, :], qsb[:, :]), deps=evs + [st_prev[0]])
            last = None
            for h in range(NH):
                last = s.dma("pool", vt[tt, h, :, :], vb[:, h * 128:(h + 1) * 128], deps=[cv])
            st_prev[0] = last

        if layer == 0:
            s.dma("sp", c0s[:, :], C0[:, :])
            s.dma("sp", s0s[:, :], S0[:, :])
            ldn = s.dma("sp", nws[:, :], qnw[:, :])
            rope_deps[0] = [ldn]
            proj_phase(wqT, lambda tt, evs: qk_post(tt, evs, qt))
            ldn2 = s.dma("sp", nws[:, :], knw[:, :], deps=[s.tok("dve")])
            rope_deps[0] = [ldn2]
            proj_phase(wkT, lambda tt, evs: qk_post(tt, evs, kt))
        else:
            proj_phase(wqT, lambda tt, evs: qk_post(tt, evs, qt))
            proj_phase(wkT, lambda tt, evs: qk_post(tt, evs, kt))
        proj_phase(wvT, v_post)
        finish(nc, s, sems)
    return nc



def l1_allowed(kb, qb):
    return (kb <= qb) and ((qb - kb) <= 16 or kb < 2 or kb % 4 == 0)


def attn_plan(layer):
    plans, bounds = [], []
    nslot = 0
    for g in range(4):
        steps, bnd = [], []
        if layer == 0:
            for kt in range(16 * g + 16):
                if kt < 16 * g:
                    ops = [(j, "full", None) for j in range(4)]
                else:
                    j0 = (kt - 16 * g) // 4
                    ops = []
                    for j in range(j0, 4):
                        if j == j0:
                            ops.append((j, "mask", kt - 16 * g))
                        else:
                            ops.append((j, "full", None))
                steps.append(dict(kt=kt, jlo=ops[0][0], jhi=ops[-1][0], ops=ops))
            bnd = [(0, 16 * g + 4 * j + 3) for j in range(4)]
            nslot = 16
        else:
            slots = {}
            for kt in range(NT):
                ops = []
                for j in range(4):
                    subs = []
                    for q in range(4):
                        qt = 16 * g + 4 * j + q
                        subs.append([[l1_allowed(2 * kt + a, 2 * qt + b) for b in (0, 1)]
                                     for a in (0, 1)])
                    flat = [v for sub in subs for row in sub for v in row]
                    if not any(flat):
                        continue
                    if all(flat):
                        ops.append((j, "full", None))
                    else:
                        key = tuple(tuple(map(tuple, sub)) for sub in subs)
                        if key not in slots:
                            slots[key] = len(slots)
                        ops.append((j, "mask", slots[key]))
                if ops:
                    steps.append(dict(kt=kt, jlo=ops[0][0], jhi=ops[-1][0], ops=ops))
            for j in range(4):
                kl = [stp["kt"] for stp in steps if any(jj == j for jj, _, _ in stp["ops"])]
                bnd.append((kl[0], kl[-1]))
            nslot = max(nslot, len(slots))
            steps_slots = slots
        plans.append(steps)
        bounds.append(bnd)
    return plans, bounds, nslot


def build_masks(layer, qq):
    plans, bounds, nslot = attn_plan(layer)
    M = np.zeros((4, max(nslot, 1), 128, 128), dtype=np.float32)
    p = np.arange(128)[:, None]
    f = np.arange(128)[None, :]
    for g in range(4):
        for stp in plans[g]:
            for (j, cls, slot) in stp["ops"]:
                if cls != "mask":
                    continue
                kt = stp["kt"]
                qt = 16 * g + 4 * j + qq
                if layer == 0:
                    r = (kt - 16 * g) % 4
                    if r < qq:
                        M[g, slot] = 0.0
                    elif r > qq:
                        M[g, slot] = NEG
                    else:
                        M[g, slot] = np.where(p <= f, 0.0, NEG)
                else:
                    for a in (0, 1):
                        for b_ in (0, 1):
                            val = 0.0 if l1_allowed(2 * kt + a, 2 * qt + b_) else NEG
                            M[g, slot, a * 64:(a + 1) * 64, b_ * 64:(b_ + 1) * 64] = val
    return M



def build_attn(nc, layer):
    plans, bounds, nslot = attn_plan(layer)
    qtg = nc.dram_tensor("qtg", [NH, TPC, 128, 128], BF16, kind="ExternalInput").ap()
    ktg = nc.dram_tensor("ktg", [NH, NT, 128, 128], BF16, kind="ExternalInput").ap()
    vtg = nc.dram_tensor("vtg", [NT, NH, 128, 128], BF16, kind="ExternalInput").ap()
    woT = nc.dram_tensor("woT", [2048, 2048], BF16, kind="ExternalInput").ap()
    x = nc.dram_tensor("x", [2048, 2048], F32, kind="ExternalInput").ap()
    identb = nc.dram_tensor("identb", [128, 128], BF16, kind="ExternalInput").ap()
    cmask = nc.dram_tensor("cmask", [4, max(nslot, 1), 128, 128], F32, kind="ExternalInput").ap()
    y = nc.dram_tensor("y", [2048, 2048], F32, kind="ExternalOutput").ap()

    s = Sched()
    with ExitStack() as st:
        sems = {e: st.enter_context(nc.semaphore(f"tk_{e}")) for e in ENGINES}
        sb = lambda name, shp, dt=F32: st.enter_context(nc.sbuf_tensor(name, shp, dt))
        QT = sb("QT", [128, NH * TPC * 128], BF16)
        WO = sb("WO", [128, 16 * 2048], BF16)
        idb = sb("idb", [128, 128], BF16)
        ktb = [sb(f"ktb{i}", [128, 128], BF16) for i in range(4)]
        vtb = [sb(f"vtb{i}", [128, 132], BF16) for i in range(4)]
        tn = [sb(f"tn{i}", [128, 512]) for i in range(2)]
        pb = [sb(f"pb{i}", [128, 512], BF16) for i in range(2)]
        tmq = sb("tmq", [128, 128])
        cm = sb("cm", [128, max(nslot, 1) * 128])
        attn = [sb(f"attn{i}", [128, 2048], BF16) for i in range(4)]
        atT = sb("atT", [128, 2048], BF16)
        ao = sb("ao", [128, 2048])
        sq = sb("sqb", [128, 2048])
        xs = [sb(f"xsb{i}", [128, 2048]) for i in range(2)]
        ysb = sb("ysb", [128, 2048])
        mm = sb("mmb", [128, 2])
        vv = sb("vvb", [128, 2])
        rl = sb("rl", [128, 4])
        sp = [st.enter_context(nc.psum_tensor(f"sp{i}", [128, 512], F32)) for i in range(2)]
        ctx = [st.enter_context(nc.psum_tensor(f"ctx{i}", [128, 512], F32)) for i in range(4)]
        trb = [st.enter_context(nc.psum_tensor(f"trb{i}", [128, 1024], BF16)) for i in range(2)]

        ldi = s.dma("sp", idb[:, :], identb[:, :])
        ones = [s.op("dve", lambda e, i=i: e.memset(vtb[i][:, :], 1.0)) for i in range(4)]
        for h in range(NH):
            for tt in range(TPC):
                s.dma("sp", QT[:, (h * TPC + tt) * 128:(h * TPC + tt + 1) * 128], qtg[h, tt, :, :])
        for k in range(16):
            s.dma("sp", WO[:, k * 2048:(k + 1) * 2048], woT[k * 128:(k + 1) * 128, :])
        ld_all = s.tok("sp")

        cm_war = None
        attn_war = [None] * 4
        xs_war = [None, None]
        st_prev = None
        for g in range(4):
            steps = plans[g]
            ldm = None
            for sl in range(nslot):
                ldm = s.dma("sp", cm[:, sl * 128:(sl + 1) * 128], cmask[g, sl, :, :], deps=[cm_war])
            kt_war = [None] * 4
            vt_war = [None] * 4
            sp_war = [None, None]
            pb_war = [None, None]
            ctx_free = [None] * 4
            for h in range(NH):
                ctx_done = [None] * 4
                for idx, stp in enumerate(steps):
                    kt, jlo, jhi = stp["kt"], stp["jlo"], stp["jhi"]
                    r = idx % 4
                    bk = idx % 2
                    ldk = s.dma("sp", ktb[r][:, :], ktg[h, kt, :, :], deps=[kt_war[r], ld_all])
                    ldv = s.dma("sp", vtb[r][:, 0:128], vtg[kt, h, :, :], deps=[vt_war[r], ones[r]])
                    span0, span1 = jlo * 128, (jhi + 1) * 128
                    mm_s = s.op("pe", lambda e, g=g, h=h, bk=bk, r=r, span0=span0, span1=span1: e.matmul(
                        sp[bk][:, span0:span1], ktb[r][:, :],
                        QT[:, (h * TPC + 4 * g) * 128 + span0:(h * TPC + 4 * g) * 128 + span1],
                        start=True, stop=True),
                        deps=[ldk, sp_war[bk]])
                    kt_war[r] = mm_s
                    exp_toks = {}
                    if layer == 0:
                        t_t = s.op("act", lambda e, bk=bk, span0=span0, span1=span1: e.activation(
                            tn[bk][:, span0:span1], sp[bk][:, span0:span1], AF.Tanh,
                            scale=SCALE_ATTN / CAP), deps=[mm_s])
                        sp_read = [t_t]
                        full_js = [j for (j, cls, _) in stp["ops"] if cls == "full"]
                        prev_mask = None
                        for (j, cls, slot) in stp["ops"]:
                            if cls == "mask":
                                d_t = s.op("dve", lambda e, bk=bk, j=j, slot=slot: e.scalar_tensor_tensor(
                                    tmq[:, :], tn[bk][:, j * 128:(j + 1) * 128], CAP,
                                    cm[:, slot * 128:(slot + 1) * 128], ALU.mult, ALU.add),
                                    deps=[t_t, ldm, prev_mask, pb_war[bk]])
                                e_t = s.op("act", lambda e, bk=bk, j=j: e.activation(
                                    pb[bk][:, j * 128:(j + 1) * 128], tmq[:, :], AF.Exp), deps=[d_t])
                                prev_mask = e_t
                                exp_toks[j] = e_t
                        if full_js:
                            f0, f1 = full_js[0], full_js[-1]
                            e_t = s.op("act", lambda e, bk=bk, f0=f0, f1=f1: e.activation(
                                pb[bk][:, f0 * 128:(f1 + 1) * 128], tn[bk][:, f0 * 128:(f1 + 1) * 128],
                                AF.Exp, scale=CAP), deps=[t_t, pb_war[bk]])
                            for j in full_js:
                                exp_toks[j] = e_t
                    else:
                        sp_read = []
                        prev_mask = None
                        full_js = [j for (j, cls, _) in stp["ops"] if cls == "full"]
                        for (j, cls, slot) in stp["ops"]:
                            if cls == "mask":
                                d_t = s.op("dve", lambda e, bk=bk, j=j, slot=slot: e.scalar_tensor_tensor(
                                    tmq[:, :], sp[bk][:, j * 128:(j + 1) * 128], SCALE_ATTN,
                                    cm[:, slot * 128:(slot + 1) * 128], ALU.mult, ALU.add),
                                    deps=[mm_s, ldm, prev_mask, pb_war[bk]])
                                e_t = s.op("act", lambda e, bk=bk, j=j: e.activation(
                                    pb[bk][:, j * 128:(j + 1) * 128], tmq[:, :], AF.Exp), deps=[d_t])
                                prev_mask = e_t
                                exp_toks[j] = e_t
                                sp_read.append(d_t)
                        if full_js:
                            f0, f1 = full_js[0], full_js[-1]
                            e_t = s.op("act", lambda e, bk=bk, f0=f0, f1=f1: e.activation(
                                pb[bk][:, f0 * 128:(f1 + 1) * 128], sp[bk][:, f0 * 128:(f1 + 1) * 128],
                                AF.Exp, scale=SCALE_ATTN), deps=[mm_s, pb_war[bk]])
                            for j in full_js:
                                exp_toks[j] = e_t
                            sp_read.append(e_t)
                    sp_war[bk] = sp_read[-1] if sp_read else exp_toks[max(exp_toks)]
                    last_ctx = None
                    for (j, cls, slot) in stp["ops"]:
                        first, last = bounds[g][j]
                        c_t = s.op("pe", lambda e, j=j, bk=bk, r=r, start=(kt == first), stop=(kt == last): e.matmul(
                            ctx[j][:, 0:129], pb[bk][:, j * 128:(j + 1) * 128], vtb[r][:, 0:129],
                            start=start, stop=stop),
                            deps=[exp_toks[j], ldv, ctx_free[j]],
                            inc=1)
                        ctx_free[j] = None
                        last_ctx = c_t
                        if kt == last:
                            ctx_done[j] = c_t
                    vt_war[r] = last_ctx
                    pb_war[bk] = last_ctx
                for j in range(4):
                    r_t = s.op("dve", lambda e, j=j: e.reciprocal(rl[:, j:j + 1], ctx[j][:, 128:129]),
                               deps=[ctx_done[j]])
                    n_t = s.op("dve", lambda e, j=j, h=h: e.tensor_scalar_mul(
                        attn[j][:, h * 128:(h + 1) * 128], ctx[j][:, 0:128], rl[:, j:j + 1]),
                        deps=[r_t, attn_war[j]])
                    ctx_free[j] = n_t
            cm_war = s.tok("dve")
            for j in range(4):
                tt = 4 * g + j
                b = j % 2
                ldx = s.dma("sp", xs[b][:, :], x[tt * 128:(tt + 1) * 128, :], deps=[xs_war[b]])
                tr_ev = [None, None]
                evs = []
                for k in range(16):
                    bk = k % 2
                    tp = s.op("pe", lambda e, j=j, k=k, bk=bk: e.transpose(
                        trb[bk][:, 0:128], attn[j][:, k * 128:(k + 1) * 128], idb[:, :]),
                        deps=[ctx_free[j], ldi, tr_ev[bk]])
                    ev = s.op("dve", lambda e, k=k, bk=bk: e.tensor_copy(atT[:, k * 128:(k + 1) * 128],
                                                                         trb[bk][:, 0:128]), deps=[tp])
                    tr_ev[bk] = ev
                    evs.append(ev)
                attn_war[j] = s.tok("pe")
                ao_evs = []
                for n in range(4):
                    bkp = n % 2
                    tokmm = None
                    for k in range(16):
                        tokmm = s.op("pe", lambda e, n=n, k=k, bkp=bkp: e.matmul(
                            sp[bkp][:, :], atT[:, k * 128:(k + 1) * 128],
                            WO[:, k * 2048 + n * 512:k * 2048 + n * 512 + 512],
                            start=(k == 0), stop=(k == 15)),
                            deps=[evs[-1], sp_war[bkp], ld_all] if k == 0 else [],
                            inc=(1 if k == 15 else 0))
                    ev = s.op("dve", lambda e, n=n, bkp=bkp: e.tensor_copy(
                        ao[:, n * 512:(n + 1) * 512], sp[bkp][:, :]), deps=[tokmm])
                    sp_war[bkp] = ev
                    ao_evs.append(ev)
                tln = emit_ln(s, ao, ysb, mm, vv, sq, 2048, EPS_LN, ao_evs + [st_prev])
                y_t = s.op("dve", lambda e, b=b: e.tensor_add(ysb[:, :], ysb[:, :], xs[b][:, :]),
                           deps=[tln, ldx])
                xs_war[b] = y_t
                st_prev = s.dma("pool", y[tt * 128:(tt + 1) * 128, :], ysb[:, :], deps=[y_t])
        finish(nc, s, sems)
    return nc



def build_ffn(nc, layer):
    x = nc.dram_tensor("x", [2048, 2048], F32, kind="ExternalInput").ap()
    wg = nc.dram_tensor("wg", [64, 128, 2048], BF16, kind="ExternalInput").ap()
    wu = nc.dram_tensor("wu", [64, 128, 2048], BF16, kind="ExternalInput").ap()
    wd = nc.dram_tensor("wd", [8192, 2048], BF16, kind="ExternalInput").ap()
    identf = nc.dram_tensor("identf", [128, 128], F32, kind="ExternalInput").ap()
    y = nc.dram_tensor("y", [2048, 2048], F32, kind="ExternalOutput").ap()

    s = Sched()
    with ExitStack() as st:
        sems = {e: st.enter_context(nc.semaphore(f"tk_{e}")) for e in ENGINES}
        sb = lambda name, shp, dt=F32: st.enter_context(nc.sbuf_tensor(name, shp, dt))
        xs = [sb(f"xs{i}", [128, 2048]) for i in range(2)]
        xn = sb("xn", [128, 2048])
        sq = sb("sqs", [128, 2048])
        mm = sb("mm", [128, 2])
        vv = sb("vvs", [128, 2])
        idf = sb("idf", [128, 128])
        xnT = sb("xnT", [128, 8 * 2048], BF16)
        hs = sb("hs", [128, 64 * 512], BF16)
        hsb = [sb(f"hsb{i}", [128, 512]) for i in range(4)]
        scr = [sb(f"scr{i}", [128, 512]) for i in range(10)]
        wgs = [sb(f"wgs{i}", [128, 2048], BF16) for i in range(2)]
        wus = [sb(f"wus{i}", [128, 2048], BF16) for i in range(2)]
        wds = [sb(f"wds{i}", [128, 2048], BF16) for i in range(2)]
        dsb = sb("dsb", [128, 2048])
        ysb = sb("ysb", [128, 2048])
        ps = [st.enter_context(nc.psum_tensor(f"ps{i}", [128, 512], F32)) for i in range(8)]

        ldi = s.dma("sp", idf[:, :], identf[:, :])
        st_prev = None
        wg_war = [None, None]
        wu_war = [None, None]
        wd_war = [None, None]
        xs_war = [None, None]
        for half in range(2):
            pe_prev = s.tok("pe")
            tr_evac = [getattr(s, 'dn_war0', None), getattr(s, 'dn_war1', None)]
            for tl in range(8):
                tt = half * 8 + tl
                b = tl % 2
                ldx = s.dma("sp", xs[b][:, :], x[tt * 128:(tt + 1) * 128, :], deps=[xs_war[b]])
                tln = emit_ln(s, xs[b], xn, mm, vv, sq, 2048, EPS_LN, [ldx, pe_prev])
                xs_war[b] = tln
                for k in range(16):
                    bk = k % 2
                    tp = s.op("pe", lambda e, k=k, bk=bk: e.transpose(
                        ps[bk][:, 0:128], xn[:, k * 128:(k + 1) * 128], idf[:, :]),
                        deps=[tln, ldi, tr_evac[bk]])
                    ev = s.op("dve", lambda e, tl=tl, k=k, bk=bk: e.tensor_copy(
                        xnT[:, k * 1024 + tl * 128: k * 1024 + (tl + 1) * 128], ps[bk][:, 0:128]),
                        deps=[tp])
                    tr_evac[bk] = ev
                    if k == 15:
                        pe_prev = tp
            allx = s.tok("dve")
            for tg in range(2):
                hs_war = getattr(s, 'hs_war_tok', None)
                hstoks = []
                hsb_war = getattr(s, 'hsb_war', [None] * 4)
                spar_prev = getattr(s, 'spar_prev', None)
                for G in range(16):
                    jt = []
                    for j in range(4):
                        idx = j * 16 + G
                        bw = idx % 2
                        ldg = s.dma("sp", wgs[bw][:, :], wg[idx, :, :], deps=[wg_war[bw]])
                        ldu = s.dma("sp", wus[bw][:, :], wu[idx, :, :], deps=[wu_war[bw]])
                        g_t = None
                        u_t = None
                        for k in range(16):
                            g_t = s.op("pe", lambda e, k=k, bw=bw, tg=tg: e.matmul(
                                ps[6][:, :], wgs[bw][:, k * 128:(k + 1) * 128],
                                xnT[:, k * 1024 + tg * 512: k * 1024 + tg * 512 + 512],
                                start=(k == 0), stop=(k == 15)),
                                deps=[allx, ldg, getattr(s, 'silu_war', None), getattr(s, 'dn_war6', None)] if k == 0 else [],
                                inc=(1 if k == 15 else 0))
                        for k in range(16):
                            u_t = s.op("pe", lambda e, k=k, bw=bw, tg=tg: e.matmul(
                                ps[7][:, :], wus[bw][:, k * 128:(k + 1) * 128],
                                xnT[:, k * 1024 + tg * 512: k * 1024 + tg * 512 + 512],
                                start=(k == 0), stop=(k == 15)),
                                deps=[ldu, getattr(s, 'mul_war', None), getattr(s, 'dn_war7', None)] if k == 0 else [],
                                inc=(1 if k == 15 else 0))
                        wg_war[bw] = g_t
                        wu_war[bw] = u_t
                        sl_t = s.op("act", lambda e, j=j: e.activation(hsb[j][:, :], ps[6][:, :], AF.Silu),
                                    deps=[g_t, hsb_war[j]])
                        s.silu_war = sl_t
                        if layer == 0:
                            mu_t = s.op("dve", lambda e, j=j: e.tensor_mul(hsb[j][:, :], hsb[j][:, :],
                                                                           ps[7][:, :]), deps=[sl_t, u_t])
                        else:
                            mu_t = s.op("dve", lambda e, j=j, idx=idx: e.tensor_tensor(
                                hs[:, idx * 512:(idx + 1) * 512], hsb[j][:, :], ps[7][:, :], ALU.mult),
                                deps=[sl_t, u_t, hs_war])
                        s.mul_war = mu_t
                        if layer != 0:
                            hsb_war[j] = mu_t
                        jt.append(mu_t)
                    if layer == 0:
                        d = jt
                        ab = []
                        for j in range(4):
                            ab.append(s.op("act", lambda e, j=j: e.activation(
                                scr[j][:, :], hsb[j][:, :], AF.Abs), deps=d + [spar_prev]))
                        cmps = [(1, 0), (2, 0), (3, 0), (2, 1), (3, 1), (3, 2)]
                        gt = {}
                        for ci, (a, b2) in enumerate(cmps):
                            gt[(a, b2)] = s.op("dve", lambda e, ci=ci, a=a, b2=b2: e.tensor_tensor(
                                scr[4 + ci][:, :], scr[a][:, :], scr[b2][:, :], ALU.is_gt),
                                deps=[ab[a], ab[b2]])
                        V_, K_ = scr[0], scr[1]

                        def keep_and_apply(j, ktok):
                            G_ = G
                            idx = j * 16 + G_
                            return s.op("dve", lambda e, j=j, idx=idx: e.tensor_mul(
                                hs[:, idx * 512:(idx + 1) * 512], hsb[j][:, :], K_[:, :]),
                                deps=[ktok, hs_war])

                        g_ = gt
                        t1 = s.op("dve", lambda e: e.tensor_add(V_[:, :], scr[4][:, :], scr[5][:, :]),
                                  deps=[g_[(1, 0)], g_[(2, 0)], ab[0]])
                        t2 = s.op("dve", lambda e: e.tensor_add(V_[:, :], V_[:, :], scr[6][:, :]),
                                  deps=[t1, g_[(3, 0)]])
                        k0 = s.op("dve", lambda e: e.tensor_scalar(K_[:, :], V_[:, :], -1.0, 2.0,
                                                                   ALU.mult, ALU.add), deps=[t2])
                        k0 = s.op("dve", lambda e: e.tensor_scalar(K_[:, :], K_[:, :], 0.0, 1.0,
                                                                   ALU.max, ALU.min), deps=[k0])
                        m0 = keep_and_apply(0, k0)
                        t1 = s.op("dve", lambda e: e.tensor_add(V_[:, :], scr[7][:, :], scr[8][:, :]),
                                  deps=[g_[(2, 1)], g_[(3, 1)], m0])
                        t2 = s.op("dve", lambda e: e.tensor_sub(V_[:, :], V_[:, :], scr[4][:, :]), deps=[t1])
                        k1 = s.op("dve", lambda e: e.tensor_scalar(K_[:, :], V_[:, :], -1.0, 1.0,
                                                                   ALU.mult, ALU.add), deps=[t2, m0])
                        k1 = s.op("dve", lambda e: e.tensor_scalar(K_[:, :], K_[:, :], 0.0, 1.0,
                                                                   ALU.max, ALU.min), deps=[k1])
                        m1 = keep_and_apply(1, k1)
                        t1 = s.op("dve", lambda e: e.tensor_sub(V_[:, :], scr[9][:, :], scr[5][:, :]),
                                  deps=[g_[(3, 2)], g_[(2, 0)], m1])
                        t2 = s.op("dve", lambda e: e.tensor_sub(V_[:, :], V_[:, :], scr[7][:, :]), deps=[t1])
                        k2 = s.op("dve", lambda e: e.tensor_scalar(K_[:, :], V_[:, :], -1.0, 0.0,
                                                                   ALU.mult, ALU.add), deps=[t2, m1])
                        k2 = s.op("dve", lambda e: e.tensor_scalar(K_[:, :], K_[:, :], 0.0, 1.0,
                                                                   ALU.max, ALU.min), deps=[k2])
                        m2 = keep_and_apply(2, k2)
                        t1 = s.op("dve", lambda e: e.tensor_add(V_[:, :], scr[6][:, :], scr[8][:, :]),
                                  deps=[g_[(3, 0)], g_[(3, 1)], m2])
                        t2 = s.op("dve", lambda e: e.tensor_add(V_[:, :], V_[:, :], scr[9][:, :]), deps=[t1])
                        k3 = s.op("dve", lambda e: e.tensor_scalar(K_[:, :], V_[:, :], 1.0, -1.0,
                                                                   ALU.mult, ALU.add), deps=[t2, m2])
                        k3 = s.op("dve", lambda e: e.tensor_scalar(K_[:, :], K_[:, :], 0.0, 1.0,
                                                                   ALU.max, ALU.min), deps=[k3])
                        m3 = keep_and_apply(3, k3)
                        hsb_war = [m0, m1, m2, m3]
                        spar_prev = m3
                        hstoks.append(m3)
                    else:
                        hstoks.extend(jt)
                s.hsb_war = hsb_war
                s.spar_prev = spar_prev
                hs_done = s.tok("dve")
                act_done = s.tok("act")
                s.hs_war_tok = None
                for pair in range(2):
                    dn_done = []
                    bar = [s.tok(e) for e in ENGINES]
                    for idx in range(64):
                        bw = idx % 2
                        ldd = s.dma("sp", wds[bw][:, :], wd[idx * 128:(idx + 1) * 128, :],
                                    deps=[wd_war[bw]])
                        last = None
                        for ti in range(2):
                            tl = pair * 2 + ti
                            for n in range(4):
                                last = s.op("pe", lambda e, idx=idx, tl=tl, ti=ti, n=n, bw=bw, tg=tg: e.matmul(
                                    ps[ti * 4 + n][:, :],
                                    hs[:, idx * 512 + tl * 128: idx * 512 + (tl + 1) * 128],
                                    wds[bw][:, n * 512:(n + 1) * 512],
                                    start=(idx == 0), stop=(idx == 63)),
                                    deps=[hs_done, act_done, ldd] + bar + ([getattr(s, f'dn_war{ti * 4 + n}', None)]
                                                           if idx == 0 else []) if (idx == 0 or n == 0 and ti == 0) else [],
                                    inc=(1 if (ti == 1 and n == 3) else 0))
                                if idx == 63:
                                    dn_done.append((ti, n, last))
                        wd_war[bw] = last
                    fin = s.tok("pe")
                    s.hs_war_tok = fin
                    bar2 = [s.tok(e) for e in ENGINES]
                    for ti in range(2):
                        tl = pair * 2 + ti
                        tt = half * 8 + 4 * tg + tl
                        b = tl % 2
                        ldx2 = s.dma("sp", xs[b][:, :], x[tt * 128:(tt + 1) * 128, :], deps=[xs_war[b]])
                        evs = []
                        for n in range(4):
                            ev = s.op("dve", lambda e, ti=ti, n=n: e.tensor_copy(
                                dsb[:, n * 512:(n + 1) * 512], ps[ti * 4 + n][:, :]), deps=[fin] + bar2)
                            setattr(s, f'dn_war{ti * 4 + n}', ev)
                            evs.append(ev)
                        tln = emit_ln(s, dsb, ysb, mm, vv, sq, 2048, EPS_LN, evs + [st_prev])
                        y_t = s.op("dve", lambda e, b=b: e.tensor_add(ysb[:, :], ysb[:, :], xs[b][:, :]),
                                   deps=[tln, ldx2])
                        xs_war[b] = y_t
                        st_prev = s.dma("pool", y[tt * 128:(tt + 1) * 128, :], ysb[:, :], deps=[y_t])
        finish(nc, s, sems)
    return nc


def run_program(build_fn, in_maps, n_cores=8, **kw):
    nc = bass.Bass("TRN2", target_bir_lowering=False)
    build_fn(nc)
    return run_bass_kernel_spmd(nc, in_maps, core_ids=list(range(n_cores)), **kw)



def _core_rows(qq):
    return np.array([(4 * i + qq) * 128 + r for i in range(TPC) for r in range(128)])


def kernel(**inp):
    inp = {k: np.asarray(v) for k, v in inp.items()}
    h = host_prep(inp)
    wb = run_castw(h)
    identf = np.eye(128, dtype=np.float32)
    identb = np.eye(128, dtype=BF)
    cores = [(c // 4, c % 4) for c in range(NCORES)]
    x = inp["L0_attn_x"]
    x_sh = [shard_rows(np.ascontiguousarray(x[b]), qq) for (b, qq) in cores]

    def spmd(build_fn, *args, maps=None):
        nc = bass.Bass("TRN2", target_bir_lowering=False)
        build_fn(nc, *args)
        return run_bass_kernel_spmd(nc, maps, core_ids=list(range(NCORES))).results

    def attn_block(layer, xs_list):
        lw = f"l{layer}"
        maps = []
        for c, (b, qq) in enumerate(cores):
            m = {"x": xs_list[c], "wqT": wb[lw + "_wqT"], "wkT": wb[lw + "_wkT"],
                 "wvT": wb[lw + "_wvT"], "identf": identf}
            if layer == 0:
                m.update(C0=h["C0"], S0=h["S0"], qnw=h["qnw"], knw=h["knw"])
            else:
                rows = _core_rows(qq)
                m["cosw"] = np.ascontiguousarray(np.tile(h["cos1"][rows], (1, NH)))
                m["sinw"] = np.ascontiguousarray(np.tile(h["sin1"][rows], (1, NH)))
            maps.append(m)
        ra = spmd(build_qkv, layer, maps=maps)
        ktg, vtg = {}, {}
        for b in range(2):
            kt_b = np.empty((NH, NT, 128, 128), dtype=BF)
            vt_b = np.empty((NT, NH, 128, 128), dtype=BF)
            for qq in range(4):
                r = ra[b * 4 + qq]
                kt_b[:, qq::4] = np.asarray(r["kt"])
                vt_b[qq::4] = np.asarray(r["vt"])
            ktg[b], vtg[b] = kt_b, vt_b
        maps = []
        for c, (b, qq) in enumerate(cores):
            maps.append({"qtg": np.asarray(ra[c]["qt"]), "ktg": ktg[b], "vtg": vtg[b],
                         "woT": wb[lw + "_woT"], "x": xs_list[c], "identb": identb,
                         "cmask": build_masks(layer, qq)})
        rb = spmd(build_attn, layer, maps=maps)
        return [np.asarray(r["y"]) for r in rb]

    def ffn_block(layer, xs_list):
        lw = f"l{layer}"
        maps = [{"x": xs_list[c], "wg": wb[lw + "_wg"], "wu": wb[lw + "_wu"],
                 "wd": wb[lw + "_wd"], "identf": identf} for c in range(NCORES)]
        rc = spmd(build_ffn, layer, maps=maps)
        return [np.asarray(r["y"]) for r in rc]

    y = attn_block(0, x_sh)
    y = ffn_block(0, y)
    y = attn_block(1, y)
    y = ffn_block(1, y)

    out = np.empty((B, T, D), dtype=np.float32)
    for b in range(2):
        out[b] = unshard_rows([y[b * 4 + qq] for qq in range(4)])
    return out
